# version 14
# baseline (speedup 1.0000x reference)
import contextlib
import numpy as np
import ml_dtypes
import sys as _sys
import concourse.bass as bass
import concourse.mybir as mybir
from concourse.bass_utils import run_bass_kernel_spmd

F32 = mybir.dt.float32
BF16 = mybir.dt.bfloat16
AF = mybir.ActivationFunctionType
ALU = mybir.AluOpType
AX = mybir.AxisListType

NCORES = 8
SEQ = 2048
C = 1024
H = 16
TKN = 128
ROWS = 4096
NT = ROWS // TKN
TPS = SEQ // TKN
CCENT = 63.5

_CFG = {"n_tiles": NT, "sublayers": ("F1", "A", "F2")}


def _build_nc():
    nc = bass.Bass("TRN2", target_bir_lowering=False)

    x_d = nc.dram_tensor("x", [ROWS, C], F32, kind="ExternalInput").ap()
    out_d = nc.dram_tensor("out", [ROWS, C], F32, kind="ExternalOutput").ap()
    dbg = {}
    if _CFG.get("debug_dump"):
        for nm, sh in [("dR", [ROWS, C]), ("dz", [ROWS, C]),
                       ("ds1", [ROWS, 1]), ("ds2", [ROWS, 1]),
                       ("dinv", [ROWS, 1])]:
            dbg[nm] = nc.dram_tensor(nm, sh, F32, kind="ExternalOutput").ap()
        dbgz = nc.dram_tensor("dzb", [ROWS, C], BF16,
                              kind="ExternalOutput").ap()
    WNAMES = ["fwrt", "fwkt", "fwvt", "wkt", "wvt", "wrt", "wot"]
    w_d = {n: nc.dram_tensor(n, [C, C], F32, kind="ExternalInput").ap()
           for n in WNAMES}
    tri_d = nc.dram_tensor("tri", [128, 128], F32, kind="ExternalInput").ap()
    idf_d = nc.dram_tensor("idf", [128, 128], BF16, kind="ExternalInput").ap()
    ones1_d = nc.dram_tensor("ones1", [1, 128], F32, kind="ExternalInput").ap()
    cju_d = nc.dram_tensor("cju", [128, 1], F32, kind="ExternalInput").ap()
    cjw_d = nc.dram_tensor("cjw", [128, 1], F32, kind="ExternalInput").ap()
    tdr_d = nc.dram_tensor("tdr", [128, H], F32, kind="ExternalInput").ap()
    tfr_d = nc.dram_tensor("tfr", [128, H], F32, kind="ExternalInput").ap()
    mu_d = {n: nc.dram_tensor(n, [128, 8], F32, kind="ExternalInput").ap()
            for n in ["murt", "mukt", "mxkt", "mxvt", "mxrt"]}

    es = contextlib.ExitStack()
    sb = lambda name, shape, dt: es.enter_context(nc.sbuf_tensor(name, shape, dt))
    ps = lambda name, shape, dt: es.enter_context(nc.psum_tensor(name, shape, dt))
    sem = lambda name: es.enter_context(nc.semaphore(name))

    w_sb = {n: sb(n + "_s", [128, 8 * C], BF16) for n in WNAMES}
    Rb = [sb("R0", [128, C], F32), sb("R1", [128, C], F32),
          sb("R2", [128, C], F32)]
    z_tm = {s: sb("z_tm" + s, [128, C], BF16) for s in ("F1", "A", "F2")}
    zf = {s: sb("zf" + s, [128, 8 * 129], BF16) for s in ("F1", "A", "F2")}
    dbuf = sb("dbuf", [128, 8 * 128], BF16)
    xr_b = sb("xr", [128, 8 * 128], BF16)
    xk_b = sb("xk", [128, 8 * 128], BF16)
    xv_b = sb("xv", [128, 8 * 128], BF16)
    junk = sb("junk", [128, C], BF16)
    def strips(tag):
        return {k: sb(k + tag, [128, 1], F32)
                for k in ("s1", "s1b", "s2", "meanb", "varb", "m2b", "lnvb",
                          "invb", "negmib")}
    lnb = {s: strips(s) for s in ("F1", "A", "F2")}
    krelu = sb("krelu", [128, C], BF16)
    ka_tm = sb("ka_tm", [128, C], BF16)
    ka_fm = sb("ka_fm", [128, 8 * 128], BF16)
    esig = sb("esig", [128, C], F32)
    denb = sb("denb", [128, C], F32)
    recbA = sb("recbA", [128, C], F32)
    recbF = sb("recbF", [128, C], F32)
    u_sb = sb("u_sb", [128, C], F32)
    ek = sb("ek", [128, C], F32)
    EKV = sb("EKV", [128, 1040], F32)
    ahat = sb("ahat", [128, 1040], F32)
    t2b = sb("t2b", [128, 1040], F32)
    numb = sb("numb", [128, 1040], F32)
    qrow = sb("qrow", [1, 1040], F32)
    Aprime = sb("Aprime", [1, 1040], F32)
    g2 = sb("g2", [128, C], BF16)
    recip16 = sb("recip16", [128, H], F32)
    r16ln = sb("r16ln", [128, H], F32)
    gfm = sb("gfm", [128, 8 * 128], BF16)
    tri = sb("tri_s", [128, 128], F32)
    idb = sb("idb_s", [128, 128], BF16)
    ones1 = sb("ones1_s", [1, 128], F32)
    cju = sb("cju_s", [128, 1], F32)
    cjw = sb("cjw_s", [128, 1], F32)
    tdr = sb("tdr_s", [128, H], F32)
    tfr = sb("tfr_s", [128, H], F32)
    wtil = sb("wtil", [128, H], F32)
    negw = sb("negw", [128, H], F32)
    utab = sb("utab", [128, H], F32)
    wtab = sb("wtab", [128, H], F32)
    ew128 = sb("ew128", [128, H], F32)
    eu16 = sb("eu16", [128, H], F32)
    mu_sb = {n: sb(n + "_s", [128, 8], F32) for n in mu_d}

    p0 = ps("p0", [128, 1024], BF16)
    pg = [ps(f"p{k}", [128, 512], F32) for k in range(1, 7)]
    p1, p2, p3, p4, p5, p6 = pg

    semT = sem("semT"); semV = sem("semV"); semS = sem("semS")
    semDX = [sem(f"semDX{k}") for k in range(3)]
    semDO = [sem(f"semDO{k}") for k in range(3)]
    semW = {n: sem("semW_" + n) for n in
            ["fwrt", "fwkt", "fwvt", "wkt", "wvt", "wrt", "wot"]}
    semDC = sem("semDC"); semDO2 = sem("semDBG")

    ops = {k: [] for k in ("t", "v", "s", "y", "g")}
    cnt = {"t": 0, "v": 0, "s": 0}
    SEMO = {"t": semT, "v": semV, "s": semS}

    def emit(stream, fn, waits=(), inc=None, grp=None):
        waits = list(waits)
        if _CFG.get("paranoid") and (
                stream in _CFG.get("paranoid_streams", ()) or
                (grp is not None and grp in _CFG.get("paranoid_grps", ()))):
            wmap = {"t": wT, "v": wV, "s": wS}
            waits += [wmap[k]() for k in _CFG.get("paranoid_which", "tvs")]
        ws = [(s_, v_) for (s_, v_) in waits if v_ > 0]
        ops[stream].append((ws, fn, inc))
        if inc is not None:
            cnt[inc] += 1

    def wT(): return (semT, cnt["t"])
    def wV(): return (semV, cnt["v"])
    def wS(): return (semS, cnt["s"])

    def ev(stream):
        return (SEMO[stream], cnt[stream])

    w3 = {n: w_sb[n][:].rearrange("p (cc d) -> p cc d", cc=8) for n in WNAMES}
    zf3 = {s: zf[s][:].rearrange("p (cc t) -> p cc t", cc=8) for s in zf}
    EKVv = EKV[:].rearrange("p (h c) -> p h c", h=H)
    ahv = ahat[:].rearrange("p (h c) -> p h c", h=H)
    t2v = t2b[:].rearrange("p (h c) -> p h c", h=H)
    numv = numb[:].rearrange("p (h c) -> p h c", h=H)
    qrv = qrow[:].rearrange("p (h c) -> p h c", h=H)
    Apv = Aprime[:].rearrange("p (h c) -> p h c", h=H)
    ekv16 = ek[:].rearrange("p (h d) -> p h d", h=H)

    def bc(ap2d, n65, parts=128):
        return ap2d.unsqueeze(2).broadcast_to([parts, ap2d.shape[1], n65])

    def pro_y(sync):
        for nm, dst, src in [("tri", tri, tri_d), ("idf", idb, idf_d),
                             ("ones1", ones1, ones1_d), ("cju", cju, cju_d),
                             ("cjw", cjw, cjw_d), ("tdr", tdr, tdr_d),
                             ("tfr", tfr, tfr_d)]:
            sync.dma_start(dst[:], src).then_inc(semDC, 16)
        for nm in mu_d:
            sync.dma_start(mu_sb[nm][:], mu_d[nm]).then_inc(semDC, 16)
    ops["y"].append(([], pro_y, None))
    NCONST = 12

    def pro_g(gpsimd):
        for nm in WNAMES:
            src = w_d[nm].rearrange("(cc p) d -> p cc d", p=128)
            gpsimd.dma_start(w3[nm], src).then_inc(semW[nm], 16)
    ops["g"].append(([], pro_g, None))

    def swtil(s_):
        s_.activation(wtil[:], tdr[:], AF.Exp)
        return s_.drain()
    emit("s", swtil, waits=[(semDC, 16 * NCONST)], inc="s")
    emit("v", lambda v: v.tensor_scalar(negw[:], wtil[:], -1.0, None, ALU.mult),
         waits=[wS()], inc="v")

    def tabs(s_):
        s_.activation(utab[:], negw[:], AF.Exp, scale=cju[:])
        s_.activation(wtab[:], negw[:], AF.Exp, scale=cjw[:])
        s_.activation(ew128[:], negw[:], AF.Exp, scale=128.0)
        s_.activation(eu16[:], tfr[:], AF.Exp)
        return s_.drain()
    emit("s", tabs, waits=[wV()], inc="s")

    def pro_v(v):
        v.tensor_scalar(w_sb["fwvt"][:], w_sb["fwvt"][:], 0.5, None, ALU.mult)
        v.memset(Aprime[:], 0.0)
        for s in zf3:
            v.memset(zf3[s][:, :, 0:1], 0.0)
        return v.drain()
    emit("v", pro_v, waits=[(semW["fwvt"], 16), (semDC, 16 * NCONST)],
         inc="v")
    EV_PRO_V = ev("v")

    n_tiles = _CFG["n_tiles"]
    def dx_wait(i):
        return (semDX[i % 3], 16 * (i // 3 + 1))

    for i in range(n_tiles):
        def load(sync, i=i):
            return sync.dma_start(Rb[i % 3][:], x_d[i * TKN:(i + 1) * TKN, :])
        waits = [(semDO[i % 3], 16 * ((i - 3) // 3 + 1))] if i >= 3 else []
        ops["y"].append((waits, lambda y, i=i: load(y, i).then_inc(
            semDX[i % 3], 16), None))

    last_rd = {id(p0): None}
    for p in pg:
        last_rd[id(p)] = None

    def rd_wait(*banks):
        ws = []
        for b in banks:
            e = last_rd[id(b)]
            if e is not None:
                ws.append(e)
        return ws

    def mark_rd(bank, event):
        last_rd[id(bank)] = event

    def twarm(t):
        last = None
        for _ in range(24):
            last = t.matmul(p1[:, 0:128], idb[:], idb[:], start=True,
                            stop=True)
        return last
    emit("t", twarm, waits=[EV_PRO_V, ev("s")], inc="t")
    emit("v", lambda v: v.drain(), waits=[ev("t"), ev("s")], inc="v")
    emit("s", lambda s_: s_.drain(), waits=[ev("t"), ev("v")], inc="s")
    emit("t", lambda t: t.nop(), waits=[ev("v"), ev("s")], inc="t")
    mark_rd(p1, ev("v"))

    track = {id(xr_b): None, id(xk_b): None, id(xv_b): None}
    for s in ("F1", "A", "F2"):
        track["v2_" + s] = None
        track["zT_" + s] = None

    def _w(*evs):
        return [e for e in evs if e is not None]

    def layer_norm(R, i, r_ev, slot, s1_ev=None):
        L = lnb[slot]
        def ssq(s_):
            return s_.activation(junk[:], R[:], AF.Square,
                                 accum_out=L["s2"][:])
        emit("s", ssq, grp="ssq", waits=_w(r_ev, track["v2_" + slot]),
             inc="s")
        ev_sq = ev("s")
        def v12(v):
            if s1_ev is None:
                v.reduce_sum(L["s1"][:], R[:], axis=AX.X)
                v.drain()
                v.tensor_scalar(L["meanb"][:], L["s1"][:], 1.0 / C, None,
                                ALU.mult)
            else:
                v.tensor_scalar(L["s1"][:], L["s1"][:], L["s1b"][:], None,
                                ALU.add)
                v.drain()
                v.tensor_scalar(L["meanb"][:], L["s1"][:], 1.0 / C, None,
                                ALU.mult)
            v.tensor_scalar(L["varb"][:], L["s2"][:], 1.0 / C, None, ALU.mult)
            v.drain()
            v.tensor_scalar(L["m2b"][:], L["meanb"][:], L["meanb"][:], None,
                            ALU.mult)
            v.drain()
            v.tensor_sub(L["varb"][:], L["varb"][:], L["m2b"][:])
            v.drain()
            return v.tensor_scalar(L["varb"][:], L["varb"][:], 1e-5, None,
                                   ALU.add)
        emit("v", v12, waits=_w(ev_sq, s1_ev, r_ev), inc="v")
        ev_v2 = ev("v")
        track["v2_" + slot] = ev_v2
        def s2f(s_):
            s_.activation(L["lnvb"][:], L["varb"][:], AF.Ln)
            s_.drain()
            return s_.activation(L["invb"][:], L["lnvb"][:], AF.Exp,
                                 scale=-0.5)
        emit("s", s2f, grp="s2f", waits=[ev_v2], inc="s")
        ev_inv = ev("s")
        def v3(v):
            v.tensor_scalar(L["negmib"][:], L["invb"][:], L["meanb"][:], -1.0,
                            ALU.mult, ALU.mult)
            v.drain()
            return v.tensor_scalar(z_tm[slot][:], R[:], L["invb"][:],
                                   L["negmib"][:], ALU.mult, ALU.add)
        emit("v", v3, waits=_w(ev_inv, track["zT_" + slot]), inc="v")
        evz = ev("v")
        if _CFG.get("debug_dump") and slot == _CFG.get("debug_slot", "F1"):
            rows = slice(i * TKN, (i + 1) * TKN)
            def dmp(s_, rows=rows):
                s_.dma_start(dbg["dR"][rows, :], R[:]).then_inc(semDO2, 16)
                s_.dma_start(dbg["ds1"][rows, :], L["s1"][:]).then_inc(
                    semDO2, 16)
                s_.dma_start(dbg["ds2"][rows, :], L["s2"][:]).then_inc(
                    semDO2, 16)
                s_.dma_start(dbg["dinv"][rows, :], L["invb"][:]).then_inc(
                    semDO2, 16)
                return s_.dma_start(dbgz[rows, :], z_tm[slot][:]).then_inc(
                    semDO2, 16)
            emit("s", dmp, waits=[evz])
        return evz

    def transpose_to(p0bank, src, ev_src, zslot=None):
        def tf(t):
            last = None
            for cc in range(8):
                last = t.transpose(p0bank[:, cc * 128:(cc + 1) * 128],
                                   src[:, cc * 128:(cc + 1) * 128], idb[:])
            return last
        emit("t", tf, waits=[ev_src, EV_PRO_V] + rd_wait(p0bank), inc="t")
        e = ev("t")
        if zslot is not None:
            track["zT_" + zslot] = e
        return e

    def evac_z(slayer, ev_t):
        def sev(s_):
            return s_.copy(zf3[slayer][:, :, 1:129],
                           p0[:].rearrange("p (cc t) -> p cc t", cc=8))
        emit("s", sev, waits=[ev_t], inc="s", grp="evac")
        e = ev("s")
        mark_rd(p0, e)
        return e

    def mixes(slayer, i, targets, ev_evac):
        z3 = zf3[slayer]
        n = i % TPS
        evs = []
        for ti, (dst, mu) in enumerate(targets):
            def vm(v, ti=ti, dst=dst, mu=mu):
                if ti == 0:
                    if n == 0 and i > 0:
                        v.memset(z3[:, :, 0:1], 0.0)
                        v.drain()
                    v.tensor_sub(dbuf[:].rearrange("p (cc t) -> p cc t", cc=8),
                                 z3[:, :, 1:129], z3[:, :, 0:128])
                    v.drain()
                last = None
                for cc in range(8):
                    last = v.scalar_tensor_tensor(
                        dst[:, cc * 128:(cc + 1) * 128],
                        dbuf[:, cc * 128:(cc + 1) * 128],
                        mu_sb[mu][:, cc:cc + 1],
                        zf[slayer][:, cc * 129:cc * 129 + 128],
                        ALU.mult, ALU.add)
                if ti == len(targets) - 1:
                    v.drain()
                    last = v.tensor_copy(z3[:, :, 0:1], z3[:, :, 128:129])
                return last
            war = [track[id(dst)]] if ti > 0 else \
                [track[id(d)] for (d, _) in targets]
            emit("v", vm, waits=_w(ev_evac, *war) if ti == 0 else
                 _w(*war), inc="v")
            evs.append(ev("v"))
        return evs

    def proj(lhsbuf, wname, pa, pb, ev_mix, extra_w=()):
        wv = w3[wname]
        def tp(t):
            last = None
            for cc in range(8):
                st = cc == 0
                sp = cc == 7
                t.matmul(pa[:], lhsbuf[:, cc * 128:(cc + 1) * 128],
                         wv[:, cc, 0:512], start=st, stop=sp)
                last = t.matmul(pb[:], lhsbuf[:, cc * 128:(cc + 1) * 128],
                                wv[:, cc, 512:1024], start=st, stop=sp)
            return last
        waits = [ev_mix, (semW[wname], 16), EV_PRO_V] + rd_wait(pa, pb) \
            + list(extra_w)
        emit("t", tp, waits=waits, inc="t")
        e = ev("t")
        if id(lhsbuf) in track:
            track[id(lhsbuf)] = e
        return e

    def ffn_a(i, R, slayer, r_ev, s1_ev=None):
        evz = layer_norm(R, i, r_ev, slayer, s1_ev)
        evt = transpose_to(p0, z_tm[slayer][:], evz, zslot=slayer)
        evev = evac_z(slayer, evt)
        evmk, evmr = mixes(slayer, i, [(xk_b, "mukt"), (xr_b, "murt")], evev)
        evk = proj(xk_b, "fwkt", p5, p6, evmk)
        def srelu(s_):
            s_.activation(krelu[:, 0:512], p5[:], AF.Relu)
            return s_.activation(krelu[:, 512:1024], p6[:], AF.Relu)
        emit("s", srelu, grp="act", waits=[evk], inc="s")
        e = ev("s")
        mark_rd(p5, e); mark_rd(p6, e)
        def vka(v):
            return v.tensor_mul(ka_tm[:], krelu[:], krelu[:])
        emit("v", vka, waits=[e], inc="v")
        evka = ev("v")
        evr = proj(xr_b, "fwrt", p5, p6, evmr)
        def sesig(s_):
            s_.activation(esig[:, 0:512], p5[:], AF.Exp, scale=-1.0)
            return s_.activation(esig[:, 512:1024], p6[:], AF.Exp,
                                 scale=-1.0)
        emit("s", sesig, grp="act", waits=[evr], inc="s")
        e2 = ev("s")
        mark_rd(p5, e2); mark_rd(p6, e2)
        def vden(v):
            return v.tensor_scalar(denb[:], esig[:], 1.0, None, ALU.add)
        emit("v", vden, waits=[e2], inc="v")
        evden = ev("v")
        def srec(s_):
            s_.activation(denb[:], denb[:], AF.Ln)
            s_.drain()
            return s_.activation(recbF[:], denb[:], AF.Exp, scale=-1.0)
        emit("s", srec, grp="rec", waits=[evden], inc="s")
        evrec = ev("s")
        return evka, evrec

    def ffn_b(i, R, evka, evrec, nslot):
        evt2 = transpose_to(p0, ka_tm[:], evka)
        def skev(s_):
            return s_.copy(ka_fm[:], p0[:])
        emit("s", skev, waits=[evt2], inc="s", grp="kevac")
        eevac = ev("s")
        mark_rd(p0, eevac)
        evq = proj(ka_fm, "fwvt", p5, p6, eevac)
        def vu(v):
            v.tensor_mul(u_sb[:, 0:512], recbF[:, 0:512], p5[:])
            v.tensor_mul(u_sb[:, 512:1024], recbF[:, 512:1024], p6[:])
            v.drain()
            return v.tensor_add(R[:], R[:], u_sb[:])
        emit("v", vu, waits=[evq, evrec], inc="v")
        e = ev("v")
        mark_rd(p5, e); mark_rd(p6, e)
        return e

    def ffn(i, R, slayer, r_ev):
        evka, evrec = ffn_a(i, R, slayer, r_ev)
        return ffn_b(i, R, evka, evrec, "A" if slayer == "F1" else "F2")

    def attn_a(i, R, r_ev, s1_ev=None):
        n = i % TPS
        evz = layer_norm(R, i, r_ev, "A", s1_ev)
        evt = transpose_to(p0, z_tm["A"][:], evz, zslot="A")
        evev = evac_z("A", evt)
        evmk, evmv, evmr = mixes(
            "A", i, [(xk_b, "mxkt"), (xv_b, "mxvt"), (xr_b, "mxrt")], evev)
        evk = proj(xk_b, "wkt", p1, p2, evmk)
        evv = proj(xv_b, "wvt", p3, p4, evmv)
        def s_ek(s_):
            s_.activation(ek[:, 0:512], p1[:], AF.Exp)
            return s_.activation(ek[:, 512:1024], p2[:], AF.Exp)
        emit("s", s_ek, grp="act", waits=[evk], inc="s")
        es_ = ev("s")
        mark_rd(p1, es_); mark_rd(p2, es_)
        def vek(v):
            if n == 0 and i > 0:
                v.memset(Aprime[:], 0.0)
            v.tensor_mul(EKVv[:, 0:8, 0:64],
                         ek[:, 0:512].rearrange("p (h d) -> p h d", h=8),
                         p3[:].rearrange("p (h d) -> p h d", h=8))
            v.tensor_mul(EKVv[:, 8:16, 0:64],
                         ek[:, 512:1024].rearrange("p (h d) -> p h d", h=8),
                         p4[:].rearrange("p (h d) -> p h d", h=8))
            v.reduce_sum(EKVv[:, :, 64], ekv16, axis=AX.X)
            v.drain()
            return v.tensor_mul(ahv, EKVv, bc(utab[:], 65))
        emit("v", vek, waits=[es_, evv], inc="v")
        evek = ev("v")
        mark_rd(p3, evek); mark_rd(p4, evek)
        evr = proj(xr_b, "wrt", p1, p2, evmr)
        def sesig(s_):
            s_.activation(esig[:, 0:512], p1[:], AF.Exp, scale=-1.0)
            return s_.activation(esig[:, 512:1024], p2[:], AF.Exp,
                                 scale=-1.0)
        emit("s", sesig, grp="act", waits=[evr], inc="s")
        e2 = ev("s")
        mark_rd(p1, e2); mark_rd(p2, e2)
        def vden(v):
            return v.tensor_scalar(denb[:], esig[:], 1.0, None, ALU.add)
        emit("v", vden, waits=[e2], inc="v")
        evden = ev("v")
        def sreca(s_):
            s_.activation(denb[:], denb[:], AF.Ln)
            s_.drain()
            return s_.activation(recbA[:], denb[:], AF.Exp, scale=-1.0)
        emit("s", sreca, grp="rec", waits=[evden], inc="s")
        return evek, ev("s")

    def attn_b(i, evek, evreca):
        PG = [p1, p2, p3, p4]
        def twkv(t):
            last = None
            for g in range(4):
                t.matmul(PG[g][:, 0:260], tri[:],
                         ahat[:, 260 * g:260 * (g + 1)],
                         start=True, stop=False)
                last = t.matmul(PG[g][:, 0:260], ones1[:],
                                Aprime[0:1, 260 * g:260 * (g + 1)],
                                start=False, stop=True)
            return last
        emit("t", twkv, waits=[evek] + rd_wait(p1, p2, p3, p4), inc="t")
        evw = ev("t")
        def vepi(v):
            for g in range(4):
                v.tensor_mul(t2v[:, 4 * g:4 * (g + 1), :],
                             PG[g][:, 0:260].rearrange("p (h c) -> p h c", h=4),
                             bc(wtab[:, 4 * g:4 * (g + 1)], 65))
            v.tensor_mul(EKVv, EKVv, bc(eu16[:], 65))
            v.drain()
            v.tensor_mul(t2v[0:1, :, :], Apv,
                         bc(wtab[0:1, :], 65, parts=1))
            v.tensor_scalar(EKVv[:, :, 64], EKVv[:, :, 64], 1e-6, None,
                            ALU.max)
            v.drain()
            v.tensor_add(numb[:], t2b[:], EKV[:])
            last = None
            for g in range(4):
                last = v.tensor_mul(
                    Apv[0:1, 4 * g:4 * (g + 1), :],
                    PG[g][0:1, 0:260].rearrange("p (h c) -> p h c", h=4),
                    bc(ew128[0:1, 4 * g:4 * (g + 1)], 65, parts=1))
            return last
        emit("v", vepi, waits=[evw], inc="v")
        evn = ev("v")
        for b in PG:
            mark_rd(b, evn)
        def srec16(s_):
            s_.activation(r16ln[:], numv[:, :, 64], AF.Ln)
            s_.drain()
            return s_.activation(recip16[:], r16ln[:], AF.Exp,
                                 scale=-1.0)
        emit("s", srec16, grp="rec", waits=[evn], inc="s")
        evr16 = ev("s")
        def vepi2(v):
            v.tensor_mul(t2v[:, :, 0:64], numv[:, :, 0:64],
                         bc(recip16[:], 64))
            v.drain()
            return v.tensor_mul(g2[:].rearrange("p (h d) -> p h d", h=H),
                                t2v[:, :, 0:64], recbA[:].rearrange(
                                    "p (h d) -> p h d", h=H))
        emit("v", vepi2, waits=[evr16, evreca], inc="v")
        return ev("v")

    def attn_d(i, R, evg):
        evt2 = transpose_to(p0, g2[:], evg)
        def sgev(s_):
            return s_.copy(gfm[:], p0[:])
        emit("s", sgev, waits=[evt2], inc="s", grp="kevac")
        egf = ev("s")
        mark_rd(p0, egf)
        evo = proj(gfm, "wot", p1, p2, egf)
        def vro(v):
            v.tensor_add(R[:, 0:512], R[:, 0:512], p1[:])
            return v.tensor_add(R[:, 512:1024], R[:, 512:1024], p2[:])
        emit("v", vro, waits=[evo], inc="v")
        e = ev("v")
        mark_rd(p1, e); mark_rd(p2, e)
        return e

    def attn(i, R, r_ev):
        evek, evreca = attn_a(i, R, r_ev, s1_ev=None)
        evg = attn_b(i, evek, evreca)
        return attn_d(i, R, evg)

    def store(i, r_ev):
        def stor(s_, i=i):
            return s_.dma_start(out_d[i * TKN:(i + 1) * TKN, :], Rb[i % 3][:])
        emit("s", lambda s_, i=i: stor(s_, i).then_inc(semDO[i % 3], 16),
             waits=_w(r_ev))

    subl = _CFG["sublayers"]
    if subl == ("F1", "A", "F2") and n_tiles >= 3:
        st_f1 = {}
        st_a = {}
        rF1 = {}
        st_f1[0] = ffn_a(0, Rb[0], "F1", r_ev=dx_wait(0))
        rF1[0] = ffn_b(0, Rb[0], *st_f1[0], nslot="A")
        st_a[0] = attn_a(0, Rb[0], r_ev=rF1[0], s1_ev=None)
        st_f1[1] = ffn_a(1, Rb[1], "F1", r_ev=dx_wait(1))
        for i in range(n_tiles):
            evg = attn_b(i, *st_a[i])
            if i + 1 < n_tiles:
                rF1[i + 1] = ffn_b(i + 1, Rb[(i + 1) % 3], *st_f1[i + 1],
                                   nslot="A")
            rA = attn_d(i, Rb[i % 3], evg)
            if i + 1 < n_tiles:
                st_a[i + 1] = attn_a(i + 1, Rb[(i + 1) % 3],
                                     r_ev=rF1[i + 1], s1_ev=None)
            st_f2 = ffn_a(i, Rb[i % 3], "F2", r_ev=rA, s1_ev=None)
            rF2 = ffn_b(i, Rb[i % 3], *st_f2, nslot="F2")
            store(i, rF2)
            if i + 2 < n_tiles:
                st_f1[i + 2] = ffn_a(i + 2, Rb[(i + 2) % 3], "F1",
                                     r_ev=dx_wait(i + 2))
    else:
        for i in range(n_tiles):
            R = Rb[i % 3]
            r_ev = dx_wait(i)
            for s in subl:
                if s == "A":
                    r_ev = attn(i, R, r_ev=r_ev)
                else:
                    r_ev = ffn(i, R, s, r_ev=r_ev)
            store(i, r_ev)

    wfin = []
    for k in range(3):
        cnt_k = (n_tiles - k + 2) // 3
        if cnt_k > 0:
            wfin.append((semDO[k], 16 * cnt_k))
    if _CFG.get("debug_dump"):
        wfin.append((semDO2, 16 * 5 * n_tiles))
    ops["y"].append((wfin, lambda y: y.nop(), None))

    with nc.Block() as block:
        @block.tensor
        def _(tensor):
            for ws, fn, inc in ops["t"]:
                for s_, v_ in ws:
                    tensor.wait_ge(s_, v_)
                inst = fn(tensor)
                if inc is not None and inst is not None:
                    inst.then_inc(SEMO[inc], 1)

        @block.vector
        def _(vector):
            for ws, fn, inc in ops["v"]:
                for s_, v_ in ws:
                    vector.wait_ge(s_, v_)
                inst = fn(vector)
                if inc is not None and inst is not None:
                    inst.then_inc(SEMO[inc], 1)

        @block.scalar
        def _(scalar):
            for ws, fn, inc in ops["s"]:
                for s_, v_ in ws:
                    scalar.wait_ge(s_, v_)
                inst = fn(scalar)
                if inc is not None and inst is not None:
                    inst.then_inc(SEMO[inc], 1)

        @block.sync
        def _(sync):
            for ws, fn, inc in ops["y"]:
                for s_, v_ in ws:
                    sync.wait_ge(s_, v_)
                fn(sync)

        @block.gpsimd
        def _(gpsimd):
            for ws, fn, inc in ops["g"]:
                for s_, v_ in ws:
                    gpsimd.wait_ge(s_, v_)
                fn(gpsimd)

    es.close()
    return nc


def kernel(L0_attn_x, L0_attn_mix_k, L0_attn_mix_v, L0_attn_mix_r,
           L0_attn_Wk, L0_attn_Wv, L0_attn_Wr, L0_attn_Wo,
           L0_attn_time_decay, L0_attn_time_first,
           L0_ffn_W_r, L0_ffn_W_k, L0_ffn_W_v, L0_ffn_mu_r, L0_ffn_mu_k):
    f32 = np.float32
    x = np.ascontiguousarray(np.asarray(L0_attn_x, dtype=f32))

    def wt(w):
        return np.ascontiguousarray(np.asarray(w, dtype=f32).T)

    weights = {
        "fwrt": wt(L0_ffn_W_r), "fwkt": wt(L0_ffn_W_k), "fwvt": wt(L0_ffn_W_v),
        "wkt": wt(L0_attn_Wk), "wvt": wt(L0_attn_Wv), "wrt": wt(L0_attn_Wr),
        "wot": wt(L0_attn_Wo),
    }

    def mutile(v):
        return np.ascontiguousarray(
            np.asarray(v, dtype=f32).reshape(-1).reshape(8, 128).T)

    mus = {
        "murt": mutile(L0_ffn_mu_r), "mukt": mutile(L0_ffn_mu_k),
        "mxkt": mutile(L0_attn_mix_k), "mxvt": mutile(L0_attn_mix_v),
        "mxrt": mutile(L0_attn_mix_r),
    }
    td = np.asarray(L0_attn_time_decay, dtype=f32).reshape(H)
    tf = np.asarray(L0_attn_time_first, dtype=f32).reshape(H)
    tdr = np.ascontiguousarray(np.broadcast_to(td[None, :], (128, H)))
    tfr = np.ascontiguousarray(np.broadcast_to(tf[None, :], (128, H)))

    j = np.arange(128, dtype=np.float64)
    consts = {
        "tri": (lambda t: (t.__setitem__((slice(None), 0), 1.0), t)[1])(
            np.triu(np.ones((128, 128), f32), 1)),
        "idf": np.eye(128, dtype=f32).astype(ml_dtypes.bfloat16),
        "ones1": np.ones((1, 128), f32),
        "cju": (CCENT - j).astype(f32).reshape(128, 1),
        "cjw": (j - 1.0 - CCENT).astype(f32).reshape(128, 1),
        "tdr": tdr, "tfr": tfr,
    }

    nc = _build_nc()
    in_maps = []
    for c in range(NCORES):
        m = {"x": np.ascontiguousarray(
            x[2 * c:2 * c + 2].reshape(ROWS, C))}
        m.update(weights)
        m.update(mus)
        m.update(consts)
        in_maps.append(m)

    res = run_bass_kernel_spmd(nc, in_maps, core_ids=list(range(NCORES)))
    out = np.stack([np.asarray(r["out"], dtype=f32) for r in res.results])
    if _CFG.get("debug_dump"):
        globals()["_DBG"] = res.results
    return out.reshape(16, 2048, 1024)



# version 15
# speedup vs baseline: 1.0275x; 1.0275x over previous
import contextlib
import numpy as np
import ml_dtypes
import sys as _sys
import concourse.bass as bass
import concourse.mybir as mybir
from concourse.bass_utils import run_bass_kernel_spmd

F32 = mybir.dt.float32
BF16 = mybir.dt.bfloat16
AF = mybir.ActivationFunctionType
ALU = mybir.AluOpType
AX = mybir.AxisListType

NCORES = 8
SEQ = 2048
C = 1024
H = 16
TKN = 128
ROWS = 4096
NT = ROWS // TKN
TPS = SEQ // TKN
CCENT = 63.5

_CFG = {"n_tiles": NT, "sublayers": ("F1", "A", "F2")}


def _build_nc():
    nc = bass.Bass("TRN2", target_bir_lowering=False)

    x_d = nc.dram_tensor("x", [ROWS, C], F32, kind="ExternalInput").ap()
    out_d = nc.dram_tensor("out", [ROWS, C], F32, kind="ExternalOutput").ap()
    dbg = {}
    if _CFG.get("debug_dump"):
        for nm, sh in [("dR", [ROWS, C]), ("dz", [ROWS, C]),
                       ("ds1", [ROWS, 1]), ("ds2", [ROWS, 1]),
                       ("dinv", [ROWS, 1])]:
            dbg[nm] = nc.dram_tensor(nm, sh, F32, kind="ExternalOutput").ap()
        dbgz = nc.dram_tensor("dzb", [ROWS, C], BF16,
                              kind="ExternalOutput").ap()
    WNAMES = ["fwrt", "fwkt", "fwvt", "wkt", "wvt", "wrt", "wot"]
    w_d = {n: nc.dram_tensor(n, [C, C], F32, kind="ExternalInput").ap()
           for n in WNAMES}
    tri_d = nc.dram_tensor("tri", [128, 128], BF16, kind="ExternalInput").ap()
    idf_d = nc.dram_tensor("idf", [128, 128], BF16, kind="ExternalInput").ap()
    ones1_d = nc.dram_tensor("ones1", [1, 128], F32, kind="ExternalInput").ap()
    cju_d = nc.dram_tensor("cju", [128, 1], F32, kind="ExternalInput").ap()
    cjw_d = nc.dram_tensor("cjw", [128, 1], F32, kind="ExternalInput").ap()
    tdr_d = nc.dram_tensor("tdr", [128, H], F32, kind="ExternalInput").ap()
    tfr_d = nc.dram_tensor("tfr", [128, H], F32, kind="ExternalInput").ap()
    mu_d = {n: nc.dram_tensor(n, [128, 8], F32, kind="ExternalInput").ap()
            for n in ["murt", "mukt", "mxkt", "mxvt", "mxrt"]}

    es = contextlib.ExitStack()
    sb = lambda name, shape, dt: es.enter_context(nc.sbuf_tensor(name, shape, dt))
    ps = lambda name, shape, dt: es.enter_context(nc.psum_tensor(name, shape, dt))
    sem = lambda name: es.enter_context(nc.semaphore(name))

    w_sb = {n: sb(n + "_s", [128, 8 * C], BF16) for n in WNAMES}
    Rb = [sb("R0", [128, C], F32), sb("R1", [128, C], F32),
          sb("R2", [128, C], F32)]
    z_tm = {s: sb("z_tm" + s, [128, C], BF16) for s in ("F1", "A", "F2")}
    zf = {s: sb("zf" + s, [128, 8 * 129], BF16) for s in ("F1", "A", "F2")}
    dbuf = sb("dbuf", [128, 8 * 128], BF16)
    xr_b = sb("xr", [128, 8 * 128], BF16)
    xk_b = sb("xk", [128, 8 * 128], BF16)
    xv_b = sb("xv", [128, 8 * 128], BF16)
    junk = sb("junk", [128, C], BF16)
    def strips(tag):
        return {k: sb(k + tag, [128, 1], F32)
                for k in ("s1", "s1b", "s2", "meanb", "varb", "m2b", "lnvb",
                          "invb", "negmib")}
    lnb = {s: strips(s) for s in ("F1", "A", "F2")}
    krelu = sb("krelu", [128, C], BF16)
    ka_tm = sb("ka_tm", [128, C], BF16)
    ka_fm = sb("ka_fm", [128, 8 * 128], BF16)
    esig = sb("esig", [128, C], F32)
    denb = sb("denb", [128, C], F32)
    recbA = sb("recbA", [128, C], F32)
    recbF = sb("recbF", [128, C], F32)
    u_sb = sb("u_sb", [128, C], F32)
    ek = sb("ek", [128, C], F32)
    EKV = sb("EKV", [128, 1040], F32)
    ahat = sb("ahat", [128, 1040], BF16)
    t2b = sb("t2b", [128, 1040], F32)
    numb = sb("numb", [128, 1040], F32)
    qrow = sb("qrow", [1, 1040], F32)
    Aprime = sb("Aprime", [1, 1040], F32)
    g2 = sb("g2", [128, C], BF16)
    recip16 = sb("recip16", [128, H], F32)
    r16ln = sb("r16ln", [128, H], F32)
    gfm = sb("gfm", [128, 8 * 128], BF16)
    tri = sb("tri_s", [128, 128], BF16)
    idb = sb("idb_s", [128, 128], BF16)
    ones1 = sb("ones1_s", [1, 128], F32)
    cju = sb("cju_s", [128, 1], F32)
    cjw = sb("cjw_s", [128, 1], F32)
    tdr = sb("tdr_s", [128, H], F32)
    tfr = sb("tfr_s", [128, H], F32)
    wtil = sb("wtil", [128, H], F32)
    negw = sb("negw", [128, H], F32)
    utab = sb("utab", [128, H], F32)
    wtab = sb("wtab", [128, H], F32)
    ew128 = sb("ew128", [128, H], F32)
    eu16 = sb("eu16", [128, H], F32)
    mu_sb = {n: sb(n + "_s", [128, 8], F32) for n in mu_d}

    p0 = ps("p0", [128, 1024], BF16)
    pg = [ps(f"p{k}", [128, 512], F32) for k in range(1, 7)]
    p1, p2, p3, p4, p5, p6 = pg

    semT = sem("semT"); semV = sem("semV"); semS = sem("semS")
    semDX = [sem(f"semDX{k}") for k in range(3)]
    semDO = [sem(f"semDO{k}") for k in range(3)]
    semW = {n: sem("semW_" + n) for n in
            ["fwrt", "fwkt", "fwvt", "wkt", "wvt", "wrt", "wot"]}
    semDC = sem("semDC"); semDO2 = sem("semDBG")

    ops = {k: [] for k in ("t", "v", "s", "y", "g")}
    cnt = {"t": 0, "v": 0, "s": 0}
    SEMO = {"t": semT, "v": semV, "s": semS}

    def emit(stream, fn, waits=(), inc=None, grp=None):
        waits = list(waits)
        if _CFG.get("paranoid") and (
                stream in _CFG.get("paranoid_streams", ()) or
                (grp is not None and grp in _CFG.get("paranoid_grps", ()))):
            wmap = {"t": wT, "v": wV, "s": wS}
            waits += [wmap[k]() for k in _CFG.get("paranoid_which", "tvs")]
        ws = [(s_, v_) for (s_, v_) in waits if v_ > 0]
        ops[stream].append((ws, fn, inc))
        if inc is not None:
            cnt[inc] += 1

    def wT(): return (semT, cnt["t"])
    def wV(): return (semV, cnt["v"])
    def wS(): return (semS, cnt["s"])

    def ev(stream):
        return (SEMO[stream], cnt[stream])

    w3 = {n: w_sb[n][:].rearrange("p (cc d) -> p cc d", cc=8) for n in WNAMES}
    zf3 = {s: zf[s][:].rearrange("p (cc t) -> p cc t", cc=8) for s in zf}
    EKVv = EKV[:].rearrange("p (h c) -> p h c", h=H)
    ahv = ahat[:].rearrange("p (h c) -> p h c", h=H)
    t2v = t2b[:].rearrange("p (h c) -> p h c", h=H)
    numv = numb[:].rearrange("p (h c) -> p h c", h=H)
    qrv = qrow[:].rearrange("p (h c) -> p h c", h=H)
    Apv = Aprime[:].rearrange("p (h c) -> p h c", h=H)
    ekv16 = ek[:].rearrange("p (h d) -> p h d", h=H)

    def bc(ap2d, n65, parts=128):
        return ap2d.unsqueeze(2).broadcast_to([parts, ap2d.shape[1], n65])

    def pro_y(sync):
        for nm, dst, src in [("tri", tri, tri_d), ("idf", idb, idf_d),
                             ("ones1", ones1, ones1_d), ("cju", cju, cju_d),
                             ("cjw", cjw, cjw_d), ("tdr", tdr, tdr_d),
                             ("tfr", tfr, tfr_d)]:
            sync.dma_start(dst[:], src).then_inc(semDC, 16)
        for nm in mu_d:
            sync.dma_start(mu_sb[nm][:], mu_d[nm]).then_inc(semDC, 16)
    ops["y"].append(([], pro_y, None))
    NCONST = 12

    def pro_g(gpsimd):
        for nm in WNAMES:
            src = w_d[nm].rearrange("(cc p) d -> p cc d", p=128)
            gpsimd.dma_start(w3[nm], src).then_inc(semW[nm], 16)
    ops["g"].append(([], pro_g, None))

    def swtil(s_):
        s_.activation(wtil[:], tdr[:], AF.Exp)
        return s_.drain()
    emit("s", swtil, waits=[(semDC, 16 * NCONST)], inc="s")
    emit("v", lambda v: v.tensor_scalar(negw[:], wtil[:], -1.0, None, ALU.mult),
         waits=[wS()], inc="v")

    def tabs(s_):
        s_.activation(utab[:], negw[:], AF.Exp, scale=cju[:])
        s_.activation(wtab[:], negw[:], AF.Exp, scale=cjw[:])
        s_.activation(ew128[:], negw[:], AF.Exp, scale=128.0)
        s_.activation(eu16[:], tfr[:], AF.Exp)
        return s_.drain()
    emit("s", tabs, waits=[wV()], inc="s")

    def pro_v(v):
        v.tensor_scalar(w_sb["fwvt"][:], w_sb["fwvt"][:], 0.5, None, ALU.mult)
        v.memset(Aprime[:], 0.0)
        for s in zf3:
            v.memset(zf3[s][:, :, 0:1], 0.0)
        return v.drain()
    emit("v", pro_v, waits=[(semW["fwvt"], 16), (semDC, 16 * NCONST)],
         inc="v")
    EV_PRO_V = ev("v")

    n_tiles = _CFG["n_tiles"]
    def dx_wait(i):
        return (semDX[i % 3], 16 * (i // 3 + 1))

    for i in range(n_tiles):
        def load(sync, i=i):
            return sync.dma_start(Rb[i % 3][:], x_d[i * TKN:(i + 1) * TKN, :])
        waits = [(semDO[i % 3], 16 * ((i - 3) // 3 + 1))] if i >= 3 else []
        ops["y"].append((waits, lambda y, i=i: load(y, i).then_inc(
            semDX[i % 3], 16), None))

    last_rd = {id(p0): None}
    for p in pg:
        last_rd[id(p)] = None

    def rd_wait(*banks):
        ws = []
        for b in banks:
            e = last_rd[id(b)]
            if e is not None:
                ws.append(e)
        return ws

    def mark_rd(bank, event):
        last_rd[id(bank)] = event

    def twarm(t):
        last = None
        for _ in range(24):
            last = t.matmul(p1[:, 0:128], idb[:], idb[:], start=True,
                            stop=True)
        return last
    emit("t", twarm, waits=[EV_PRO_V, ev("s")], inc="t")
    emit("v", lambda v: v.drain(), waits=[ev("t"), ev("s")], inc="v")
    emit("s", lambda s_: s_.drain(), waits=[ev("t"), ev("v")], inc="s")
    emit("t", lambda t: t.nop(), waits=[ev("v"), ev("s")], inc="t")
    mark_rd(p1, ev("v"))

    track = {id(xr_b): None, id(xk_b): None, id(xv_b): None}
    for s in ("F1", "A", "F2"):
        track["v2_" + s] = None
        track["zT_" + s] = None

    def _w(*evs):
        return [e for e in evs if e is not None]

    def layer_norm(R, i, r_ev, slot, s1_ev=None):
        L = lnb[slot]
        def ssq(s_):
            return s_.activation(junk[:], R[:], AF.Square,
                                 accum_out=L["s2"][:])
        emit("s", ssq, grp="ssq", waits=_w(r_ev, track["v2_" + slot]),
             inc="s")
        ev_sq = ev("s")
        def v12(v):
            if s1_ev is None:
                v.reduce_sum(L["s1"][:], R[:], axis=AX.X)
                v.drain()
                v.tensor_scalar(L["meanb"][:], L["s1"][:], 1.0 / C, None,
                                ALU.mult)
            else:
                v.tensor_scalar(L["s1"][:], L["s1"][:], L["s1b"][:], None,
                                ALU.add)
                v.drain()
                v.tensor_scalar(L["meanb"][:], L["s1"][:], 1.0 / C, None,
                                ALU.mult)
            v.tensor_scalar(L["varb"][:], L["s2"][:], 1.0 / C, None, ALU.mult)
            v.drain()
            v.tensor_scalar(L["m2b"][:], L["meanb"][:], L["meanb"][:], None,
                            ALU.mult)
            v.drain()
            v.tensor_sub(L["varb"][:], L["varb"][:], L["m2b"][:])
            v.drain()
            return v.tensor_scalar(L["varb"][:], L["varb"][:], 1e-5, None,
                                   ALU.add)
        emit("v", v12, waits=_w(ev_sq, s1_ev, r_ev), inc="v")
        ev_v2 = ev("v")
        track["v2_" + slot] = ev_v2
        def s2f(s_):
            s_.activation(L["lnvb"][:], L["varb"][:], AF.Ln)
            s_.drain()
            return s_.activation(L["invb"][:], L["lnvb"][:], AF.Exp,
                                 scale=-0.5)
        emit("s", s2f, grp="s2f", waits=[ev_v2], inc="s")
        ev_inv = ev("s")
        def v3(v):
            v.tensor_scalar(L["negmib"][:], L["invb"][:], L["meanb"][:], -1.0,
                            ALU.mult, ALU.mult)
            v.drain()
            return v.tensor_scalar(z_tm[slot][:], R[:], L["invb"][:],
                                   L["negmib"][:], ALU.mult, ALU.add)
        emit("v", v3, waits=_w(ev_inv, track["zT_" + slot]), inc="v")
        evz = ev("v")
        if _CFG.get("debug_dump") and slot == _CFG.get("debug_slot", "F1"):
            rows = slice(i * TKN, (i + 1) * TKN)
            def dmp(s_, rows=rows):
                s_.dma_start(dbg["dR"][rows, :], R[:]).then_inc(semDO2, 16)
                s_.dma_start(dbg["ds1"][rows, :], L["s1"][:]).then_inc(
                    semDO2, 16)
                s_.dma_start(dbg["ds2"][rows, :], L["s2"][:]).then_inc(
                    semDO2, 16)
                s_.dma_start(dbg["dinv"][rows, :], L["invb"][:]).then_inc(
                    semDO2, 16)
                return s_.dma_start(dbgz[rows, :], z_tm[slot][:]).then_inc(
                    semDO2, 16)
            emit("s", dmp, waits=[evz])
        return evz

    def transpose_to(p0bank, src, ev_src, zslot=None):
        def tf(t):
            last = None
            for cc in range(8):
                last = t.transpose(p0bank[:, cc * 128:(cc + 1) * 128],
                                   src[:, cc * 128:(cc + 1) * 128], idb[:])
            return last
        emit("t", tf, waits=[ev_src, EV_PRO_V] + rd_wait(p0bank), inc="t")
        e = ev("t")
        if zslot is not None:
            track["zT_" + zslot] = e
        return e

    def evac_z(slayer, ev_t):
        def sev(s_):
            return s_.copy(zf3[slayer][:, :, 1:129],
                           p0[:].rearrange("p (cc t) -> p cc t", cc=8))
        emit("s", sev, waits=[ev_t], inc="s", grp="evac")
        e = ev("s")
        mark_rd(p0, e)
        return e

    def mixes(slayer, i, targets, ev_evac):
        z3 = zf3[slayer]
        n = i % TPS
        evs = []
        for ti, (dst, mu) in enumerate(targets):
            def vm(v, ti=ti, dst=dst, mu=mu):
                if ti == 0:
                    if n == 0 and i > 0:
                        v.memset(z3[:, :, 0:1], 0.0)
                        v.drain()
                    v.tensor_sub(dbuf[:].rearrange("p (cc t) -> p cc t", cc=8),
                                 z3[:, :, 1:129], z3[:, :, 0:128])
                    v.drain()
                last = None
                for cc in range(8):
                    last = v.scalar_tensor_tensor(
                        dst[:, cc * 128:(cc + 1) * 128],
                        dbuf[:, cc * 128:(cc + 1) * 128],
                        mu_sb[mu][:, cc:cc + 1],
                        zf[slayer][:, cc * 129:cc * 129 + 128],
                        ALU.mult, ALU.add)
                if ti == len(targets) - 1:
                    v.drain()
                    last = v.tensor_copy(z3[:, :, 0:1], z3[:, :, 128:129])
                return last
            war = [track[id(dst)]] if ti > 0 else \
                [track[id(d)] for (d, _) in targets]
            emit("v", vm, waits=_w(ev_evac, *war) if ti == 0 else
                 _w(*war), inc="v")
            evs.append(ev("v"))
        return evs

    def proj(lhsbuf, wname, pa, pb, ev_mix, extra_w=()):
        wv = w3[wname]
        def tp(t):
            last = None
            for cc in range(8):
                st = cc == 0
                sp = cc == 7
                t.matmul(pa[:], lhsbuf[:, cc * 128:(cc + 1) * 128],
                         wv[:, cc, 0:512], start=st, stop=sp)
                last = t.matmul(pb[:], lhsbuf[:, cc * 128:(cc + 1) * 128],
                                wv[:, cc, 512:1024], start=st, stop=sp)
            return last
        waits = [ev_mix, (semW[wname], 16), EV_PRO_V] + rd_wait(pa, pb) \
            + list(extra_w)
        emit("t", tp, waits=waits, inc="t")
        e = ev("t")
        if id(lhsbuf) in track:
            track[id(lhsbuf)] = e
        return e

    def ffn_a(i, R, slayer, r_ev, s1_ev=None):
        evz = layer_norm(R, i, r_ev, slayer, s1_ev)
        evt = transpose_to(p0, z_tm[slayer][:], evz, zslot=slayer)
        evev = evac_z(slayer, evt)
        evmr, evmk = mixes(slayer, i, [(xr_b, "murt"), (xk_b, "mukt")], evev)
        evr = proj(xr_b, "fwrt", p5, p6, evmr)
        def sesig(s_):
            s_.activation(esig[:, 0:512], p5[:], AF.Exp, scale=-1.0)
            return s_.activation(esig[:, 512:1024], p6[:], AF.Exp,
                                 scale=-1.0)
        emit("s", sesig, grp="act", waits=[evr], inc="s")
        e2 = ev("s")
        mark_rd(p5, e2); mark_rd(p6, e2)
        def vden(v):
            return v.tensor_scalar(denb[:], esig[:], 1.0, None, ALU.add)
        emit("v", vden, waits=[e2], inc="v")
        evden = ev("v")
        def srec(s_):
            s_.activation(denb[:], denb[:], AF.Ln)
            s_.drain()
            return s_.activation(recbF[:], denb[:], AF.Exp, scale=-1.0)
        emit("s", srec, grp="rec", waits=[evden], inc="s")
        evrec = ev("s")
        evk = proj(xk_b, "fwkt", p5, p6, evmk)
        def srelu(s_):
            s_.activation(krelu[:, 0:512], p5[:], AF.Relu)
            return s_.activation(krelu[:, 512:1024], p6[:], AF.Relu)
        emit("s", srelu, grp="act", waits=[evk], inc="s")
        e = ev("s")
        mark_rd(p5, e); mark_rd(p6, e)
        def vka(v):
            return v.tensor_mul(ka_tm[:], krelu[:], krelu[:])
        emit("v", vka, waits=[e], inc="v")
        return ev("v"), evrec

    def ffn_b(i, R, evka, evrec, nslot):
        evt2 = transpose_to(p0, ka_tm[:], evka)
        def skev(s_):
            return s_.copy(ka_fm[:], p0[:])
        emit("s", skev, waits=[evt2], inc="s", grp="kevac")
        eevac = ev("s")
        mark_rd(p0, eevac)
        evq = proj(ka_fm, "fwvt", p5, p6, eevac)
        def vu(v):
            v.tensor_mul(u_sb[:, 0:512], recbF[:, 0:512], p5[:])
            v.tensor_mul(u_sb[:, 512:1024], recbF[:, 512:1024], p6[:])
            v.drain()
            return v.tensor_add(R[:], R[:], u_sb[:])
        emit("v", vu, waits=[evq, evrec], inc="v")
        e = ev("v")
        mark_rd(p5, e); mark_rd(p6, e)
        return e

    def ffn(i, R, slayer, r_ev):
        evka, evrec = ffn_a(i, R, slayer, r_ev)
        return ffn_b(i, R, evka, evrec, "A" if slayer == "F1" else "F2")

    def attn_a(i, R, r_ev, s1_ev=None):
        n = i % TPS
        evz = layer_norm(R, i, r_ev, "A", s1_ev)
        evt = transpose_to(p0, z_tm["A"][:], evz, zslot="A")
        evev = evac_z("A", evt)
        evmk, evmv, evmr = mixes(
            "A", i, [(xk_b, "mxkt"), (xv_b, "mxvt"), (xr_b, "mxrt")], evev)
        evk = proj(xk_b, "wkt", p1, p2, evmk)
        evv = proj(xv_b, "wvt", p3, p4, evmv)
        def s_ek(s_):
            s_.activation(ek[:, 0:512], p1[:], AF.Exp)
            return s_.activation(ek[:, 512:1024], p2[:], AF.Exp)
        emit("s", s_ek, grp="act", waits=[evk], inc="s")
        es_ = ev("s")
        mark_rd(p1, es_); mark_rd(p2, es_)
        def vek(v):
            if n == 0 and i > 0:
                v.memset(Aprime[:], 0.0)
            v.tensor_mul(EKVv[:, 0:8, 0:64],
                         ek[:, 0:512].rearrange("p (h d) -> p h d", h=8),
                         p3[:].rearrange("p (h d) -> p h d", h=8))
            v.tensor_mul(EKVv[:, 8:16, 0:64],
                         ek[:, 512:1024].rearrange("p (h d) -> p h d", h=8),
                         p4[:].rearrange("p (h d) -> p h d", h=8))
            v.reduce_sum(EKVv[:, :, 64], ekv16, axis=AX.X)
            v.drain()
            return v.tensor_mul(ahv, EKVv, bc(utab[:], 65))
        emit("v", vek, waits=[es_, evv], inc="v")
        evek = ev("v")
        mark_rd(p3, evek); mark_rd(p4, evek)
        evr = proj(xr_b, "wrt", p1, p2, evmr)
        def sesig(s_):
            s_.activation(esig[:, 0:512], p1[:], AF.Exp, scale=-1.0)
            return s_.activation(esig[:, 512:1024], p2[:], AF.Exp,
                                 scale=-1.0)
        emit("s", sesig, grp="act", waits=[evr], inc="s")
        e2 = ev("s")
        mark_rd(p1, e2); mark_rd(p2, e2)
        def vden(v):
            return v.tensor_scalar(denb[:], esig[:], 1.0, None, ALU.add)
        emit("v", vden, waits=[e2], inc="v")
        evden = ev("v")
        def sreca(s_):
            s_.activation(denb[:], denb[:], AF.Ln)
            s_.drain()
            return s_.activation(recbA[:], denb[:], AF.Exp, scale=-1.0)
        emit("s", sreca, grp="rec", waits=[evden], inc="s")
        return evek, ev("s")

    def attn_b(i, evek, evreca):
        PG = [p1, p2, p3, p4]
        def twkv(t):
            last = None
            for g in range(4):
                t.matmul(PG[g][:, 0:260], tri[:],
                         ahat[:, 260 * g:260 * (g + 1)],
                         start=True, stop=False)
                last = t.matmul(PG[g][:, 0:260], ones1[:],
                                Aprime[0:1, 260 * g:260 * (g + 1)],
                                start=False, stop=True)
            return last
        emit("t", twkv, waits=[evek] + rd_wait(p1, p2, p3, p4), inc="t")
        evw = ev("t")
        def vepi(v):
            for g in range(4):
                v.tensor_mul(t2v[:, 4 * g:4 * (g + 1), :],
                             PG[g][:, 0:260].rearrange("p (h c) -> p h c", h=4),
                             bc(wtab[:, 4 * g:4 * (g + 1)], 65))
            v.tensor_mul(EKVv, EKVv, bc(eu16[:], 65))
            v.drain()
            v.tensor_mul(t2v[0:1, :, :], Apv,
                         bc(wtab[0:1, :], 65, parts=1))
            v.tensor_scalar(EKVv[:, :, 64], EKVv[:, :, 64], 1e-6, None,
                            ALU.max)
            v.drain()
            v.tensor_add(numb[:], t2b[:], EKV[:])
            last = None
            for g in range(4):
                last = v.tensor_mul(
                    Apv[0:1, 4 * g:4 * (g + 1), :],
                    PG[g][0:1, 0:260].rearrange("p (h c) -> p h c", h=4),
                    bc(ew128[0:1, 4 * g:4 * (g + 1)], 65, parts=1))
            return last
        emit("v", vepi, waits=[evw], inc="v")
        evn = ev("v")
        for b in PG:
            mark_rd(b, evn)
        def srec16(s_):
            s_.activation(r16ln[:], numv[:, :, 64], AF.Ln)
            s_.drain()
            return s_.activation(recip16[:], r16ln[:], AF.Exp,
                                 scale=-1.0)
        emit("s", srec16, grp="rec", waits=[evn], inc="s")
        evr16 = ev("s")
        def vepi2(v):
            v.tensor_mul(t2v[:, :, 0:64], numv[:, :, 0:64],
                         bc(recip16[:], 64))
            v.drain()
            return v.tensor_mul(g2[:].rearrange("p (h d) -> p h d", h=H),
                                t2v[:, :, 0:64], recbA[:].rearrange(
                                    "p (h d) -> p h d", h=H))
        emit("v", vepi2, waits=[evr16, evreca], inc="v")
        return ev("v")

    def attn_d(i, R, evg):
        evt2 = transpose_to(p0, g2[:], evg)
        def sgev(s_):
            return s_.copy(gfm[:], p0[:])
        emit("s", sgev, waits=[evt2], inc="s", grp="kevac")
        egf = ev("s")
        mark_rd(p0, egf)
        evo = proj(gfm, "wot", p1, p2, egf)
        def vro(v):
            v.tensor_add(R[:, 0:512], R[:, 0:512], p1[:])
            return v.tensor_add(R[:, 512:1024], R[:, 512:1024], p2[:])
        emit("v", vro, waits=[evo], inc="v")
        e = ev("v")
        mark_rd(p1, e); mark_rd(p2, e)
        return e

    def attn(i, R, r_ev):
        evek, evreca = attn_a(i, R, r_ev, s1_ev=None)
        evg = attn_b(i, evek, evreca)
        return attn_d(i, R, evg)

    def store(i, r_ev):
        def stor(s_, i=i):
            return s_.dma_start(out_d[i * TKN:(i + 1) * TKN, :], Rb[i % 3][:])
        emit("s", lambda s_, i=i: stor(s_, i).then_inc(semDO[i % 3], 16),
             waits=_w(r_ev))

    subl = _CFG["sublayers"]
    if subl == ("F1", "A", "F2") and n_tiles >= 3:
        st_f1 = {}
        st_a = {}
        rF1 = {}
        st_f1[0] = ffn_a(0, Rb[0], "F1", r_ev=dx_wait(0))
        rF1[0] = ffn_b(0, Rb[0], *st_f1[0], nslot="A")
        st_a[0] = attn_a(0, Rb[0], r_ev=rF1[0], s1_ev=None)
        st_f1[1] = ffn_a(1, Rb[1], "F1", r_ev=dx_wait(1))
        for i in range(n_tiles):
            evg = attn_b(i, *st_a[i])
            if i + 1 < n_tiles:
                rF1[i + 1] = ffn_b(i + 1, Rb[(i + 1) % 3], *st_f1[i + 1],
                                   nslot="A")
            rA = attn_d(i, Rb[i % 3], evg)
            if i + 1 < n_tiles:
                st_a[i + 1] = attn_a(i + 1, Rb[(i + 1) % 3],
                                     r_ev=rF1[i + 1], s1_ev=None)
            st_f2 = ffn_a(i, Rb[i % 3], "F2", r_ev=rA, s1_ev=None)
            rF2 = ffn_b(i, Rb[i % 3], *st_f2, nslot="F2")
            store(i, rF2)
            if i + 2 < n_tiles:
                st_f1[i + 2] = ffn_a(i + 2, Rb[(i + 2) % 3], "F1",
                                     r_ev=dx_wait(i + 2))
    else:
        for i in range(n_tiles):
            R = Rb[i % 3]
            r_ev = dx_wait(i)
            for s in subl:
                if s == "A":
                    r_ev = attn(i, R, r_ev=r_ev)
                else:
                    r_ev = ffn(i, R, s, r_ev=r_ev)
            store(i, r_ev)

    wfin = []
    for k in range(3):
        cnt_k = (n_tiles - k + 2) // 3
        if cnt_k > 0:
            wfin.append((semDO[k], 16 * cnt_k))
    if _CFG.get("debug_dump"):
        wfin.append((semDO2, 16 * 5 * n_tiles))
    ops["y"].append((wfin, lambda y: y.nop(), None))

    with nc.Block() as block:
        @block.tensor
        def _(tensor):
            for ws, fn, inc in ops["t"]:
                for s_, v_ in ws:
                    tensor.wait_ge(s_, v_)
                inst = fn(tensor)
                if inc is not None and inst is not None:
                    inst.then_inc(SEMO[inc], 1)

        @block.vector
        def _(vector):
            for ws, fn, inc in ops["v"]:
                for s_, v_ in ws:
                    vector.wait_ge(s_, v_)
                inst = fn(vector)
                if inc is not None and inst is not None:
                    inst.then_inc(SEMO[inc], 1)

        @block.scalar
        def _(scalar):
            for ws, fn, inc in ops["s"]:
                for s_, v_ in ws:
                    scalar.wait_ge(s_, v_)
                inst = fn(scalar)
                if inc is not None and inst is not None:
                    inst.then_inc(SEMO[inc], 1)

        @block.sync
        def _(sync):
            for ws, fn, inc in ops["y"]:
                for s_, v_ in ws:
                    sync.wait_ge(s_, v_)
                fn(sync)

        @block.gpsimd
        def _(gpsimd):
            for ws, fn, inc in ops["g"]:
                for s_, v_ in ws:
                    gpsimd.wait_ge(s_, v_)
                fn(gpsimd)

    es.close()
    return nc


def kernel(L0_attn_x, L0_attn_mix_k, L0_attn_mix_v, L0_attn_mix_r,
           L0_attn_Wk, L0_attn_Wv, L0_attn_Wr, L0_attn_Wo,
           L0_attn_time_decay, L0_attn_time_first,
           L0_ffn_W_r, L0_ffn_W_k, L0_ffn_W_v, L0_ffn_mu_r, L0_ffn_mu_k):
    f32 = np.float32
    x = np.ascontiguousarray(np.asarray(L0_attn_x, dtype=f32))

    def wt(w):
        return np.ascontiguousarray(np.asarray(w, dtype=f32).T)

    weights = {
        "fwrt": wt(L0_ffn_W_r), "fwkt": wt(L0_ffn_W_k), "fwvt": wt(L0_ffn_W_v),
        "wkt": wt(L0_attn_Wk), "wvt": wt(L0_attn_Wv), "wrt": wt(L0_attn_Wr),
        "wot": wt(L0_attn_Wo),
    }

    def mutile(v):
        return np.ascontiguousarray(
            np.asarray(v, dtype=f32).reshape(-1).reshape(8, 128).T)

    mus = {
        "murt": mutile(L0_ffn_mu_r), "mukt": mutile(L0_ffn_mu_k),
        "mxkt": mutile(L0_attn_mix_k), "mxvt": mutile(L0_attn_mix_v),
        "mxrt": mutile(L0_attn_mix_r),
    }
    td = np.asarray(L0_attn_time_decay, dtype=f32).reshape(H)
    tf = np.asarray(L0_attn_time_first, dtype=f32).reshape(H)
    tdr = np.ascontiguousarray(np.broadcast_to(td[None, :], (128, H)))
    tfr = np.ascontiguousarray(np.broadcast_to(tf[None, :], (128, H)))

    j = np.arange(128, dtype=np.float64)
    consts = {
        "tri": (lambda t: (t.__setitem__((slice(None), 0), 1.0), t)[1])(
            np.triu(np.ones((128, 128), f32), 1)).astype(ml_dtypes.bfloat16),
        "idf": np.eye(128, dtype=f32).astype(ml_dtypes.bfloat16),
        "ones1": np.ones((1, 128), f32),
        "cju": (CCENT - j).astype(f32).reshape(128, 1),
        "cjw": (j - 1.0 - CCENT).astype(f32).reshape(128, 1),
        "tdr": tdr, "tfr": tfr,
    }

    nc = _build_nc()
    in_maps = []
    for c in range(NCORES):
        m = {"x": np.ascontiguousarray(
            x[2 * c:2 * c + 2].reshape(ROWS, C))}
        m.update(weights)
        m.update(mus)
        m.update(consts)
        in_maps.append(m)

    res = run_bass_kernel_spmd(nc, in_maps, core_ids=list(range(NCORES)))
    out = np.stack([np.asarray(r["out"], dtype=f32) for r in res.results])
    if _CFG.get("debug_dump"):
        globals()["_DBG"] = res.results
    return out.reshape(16, 2048, 1024)

